# version 01 (final)
# baseline (speedup 1.0000x reference)
import numpy as np
import ml_dtypes
import contextlib
import concourse.bass as bass
import concourse.mybir as mybir
from concourse.bass_utils import run_bass_kernel_spmd

F32 = mybir.dt.float32
BF16 = mybir.dt.bfloat16
I8 = mybir.dt.int8
I32 = mybir.dt.int32
AF = mybir.ActivationFunctionType
ALU = mybir.AluOpType
AX = mybir.AxisListType

B, L, H = 2, 16384, 2048
D, S, R, FI = 4096, 16, 128, 8192
NC = 8
TLOC = L // 4
XPAD = TLOC + 128
NW = 2
WTOK = 2176
WCORE = 2048
NDT, NFT, NIT, NHT = D // 128, H // 128, FI // 128, H // 128
TB2 = 1024
NT2 = TLOC // TB2
NG = 16
DTG = NDT // NG
FFB = 512
NFB = TLOC // FFB
NE8 = 8
P3B = 256
NP3 = TLOC // P3B

RG_A = [[0, 1], [4, 5], [2, 3], [6, 7]]
RG_B = [[1, 2], [5, 6], [0, 3], [4, 7]]
RG_C = [[2, 3], [6, 7], [0, 1], [4, 5]]


def _bf(x):
    return np.ascontiguousarray(np.asarray(x).astype(ml_dtypes.bfloat16))


def stage_inputs(ins):
    x = np.asarray(ins["L0_attn_x"], np.float32)
    convw = np.asarray(ins["L0_attn_conv_w"], np.float32)
    convb = np.asarray(ins["L0_attn_conv_b"], np.float32)
    bdt = np.asarray(ins["L0_attn_b_dt"], np.float32)
    Alog = np.asarray(ins["L0_attn_A_log"], np.float32)
    Dv = np.asarray(ins["L0_attn_D"], np.float32)
    wg8 = np.asarray(ins["L0_ffn_w_gate"], np.int8)
    wu8 = np.asarray(ins["L0_ffn_w_up"], np.int8)
    wd8 = np.asarray(ins["L0_ffn_w_down"], np.int8)

    shared = {
        "Win": np.ascontiguousarray(np.asarray(ins["L0_attn_W_in"], np.float32)),
        "Wx": np.ascontiguousarray(np.asarray(ins["L0_attn_W_x"], np.float32)),
        "Wdt": np.ascontiguousarray(np.asarray(ins["L0_attn_W_dt"], np.float32)),
        "Wout": np.ascontiguousarray(np.asarray(ins["L0_attn_W_out"], np.float32)),
        "convw": np.ascontiguousarray(convw[:, 0, :].reshape(NDT, 128, 3).transpose(1, 0, 2).reshape(128, NDT * 3)),
        "convb": np.ascontiguousarray(convb.reshape(NDT, 128).T),
        "bdt": np.ascontiguousarray(bdt.reshape(NDT, 128).T),
        "Dvec": np.ascontiguousarray(Dv.reshape(NDT, 128).T),
        "AlogT": np.ascontiguousarray(Alog.reshape(NDT, 128, S).transpose(1, 0, 2).reshape(128, NDT * S)),
        "wg8r": np.ascontiguousarray(wg8.reshape(NFT, 128, NIT, 128).transpose(2, 1, 0, 3).reshape(NIT, 128 * NFT * 128)),
        "wu8r": np.ascontiguousarray(wu8.reshape(NFT, 128, NIT, 128).transpose(2, 1, 0, 3).reshape(NIT, 128 * NFT * 128)),
        "wd8r": np.ascontiguousarray(wd8.reshape(NE8, 8, 128, NHT, 128).transpose(3, 0, 2, 1, 4).reshape(NHT * NE8, 128 * 8 * 128)),
        "sgate": np.ascontiguousarray(np.asarray(ins["L0_ffn_s_gate"], np.float32).reshape(NIT, 128).T),
        "sup": np.ascontiguousarray(np.asarray(ins["L0_ffn_s_up"], np.float32).reshape(NIT, 128).T),
        "sdown": np.ascontiguousarray(np.asarray(ins["L0_ffn_s_down"], np.float32).reshape(NHT, 128).T),
        "identbf": _bf(np.eye(128, dtype=np.float32)),
    }
    ind = np.zeros((32, 32 * 128), np.float32)
    for j in range(32):
        ind[j, j * 128:(j + 1) * 128] = 1.0
    shared["indall"] = _bf(ind)

    in_maps = []
    for k in range(NC):
        b, q = k // 4, k % 4
        xp = np.zeros((XPAD, H), np.float32)
        g0 = q * TLOC - 1
        lo, hi = max(0, g0), min(L, g0 + XPAD)
        xp[lo - g0:hi - g0, :] = x[b, lo:hi, :]
        cfgf = np.zeros((128, 4), np.float32)
        if q == 1: cfgf[:, 0] = 1.0
        if q == 2: cfgf[:, 1] = 1.0
        if q == 3: cfgf[:, 2] = 1.0
        cfgi = np.zeros((1, 8), np.int32)
        cfgi[0, 0] = 16 if q == 0 else 0
        cfgi[0, 1] = 16 if q == 1 else 0
        cfgi[0, 2] = 16 if q == 2 else 0
        cfgi[0, 3] = 1 if q == 1 else 0
        cfgi[0, 4] = 1 if q == 2 else 0
        cfgi[0, 5] = 1 if q == 3 else 0
        m = {"x": xp, "cfgf": cfgf, "cfgi": cfgi}
        m.update(shared)
        in_maps.append(m)
    return in_maps


class Plan:
    AUTO = ("vector", "scalar")

    def __init__(self, nc):
        self.ops = {e: [] for e in ("tensor", "vector", "scalar", "gpsimd", "sync")}
        self.auto = {e: bass_alloc_auto(nc, e) for e in self.AUTO}
        self.auton = {e: 0 for e in self.AUTO}
        self.prev_mark = {e: None for e in self.AUTO}
        self._marks = {}

    def op(self, eng, method, *args, waits=(), incs=(), **kwargs):
        ws = []
        for (h, v) in waits:
            if isinstance(v, int) and v <= 0:
                continue
            ws.append((h, v))
        incs = list(incs)
        if eng in self.AUTO and method not in ("wait_ge", "dma_start"):
            if self.prev_mark[eng] is not None:
                ws.append(self.prev_mark[eng])
            if incs:
                h, v = incs[0]
                self._marks.setdefault(id(h), [h, 0])
                rec = self._marks[id(h)]
                rec[1] += (v if isinstance(v, int) else 1)
                self.prev_mark[eng] = (rec[0], rec[1])
            else:
                self.auton[eng] += 1
                incs.append((self.auto[eng], 1))
                self.prev_mark[eng] = (self.auto[eng], self.auton[eng])
        self.ops[eng].append((tuple(ws), method, args, kwargs, tuple(incs)))

    def replay(self, eng_name, eng):
        for waits, method, args, kwargs, incs in self.ops[eng_name]:
            for (h, v) in waits:
                eng.wait_ge(h, v)
            inst = getattr(eng, method)(*args, **kwargs)
            for (h, v) in incs:
                if v is None:
                    inst.then_inc(h)
                else:
                    inst.then_inc(h, v)


def bass_alloc_auto(nc, e):
    return nc.alloc_semaphore(f"auto_{e}")


class Ctr:
    def __init__(self, nc, name):
        self.h = nc.alloc_semaphore(name)
        self.n = 0

    def nxt(self, k=1):
        self.n += k
        return (self.h, k)

    def at(self):
        return (self.h, self.n)


class Pipe:
    def __init__(self, depth):
        self.depth = depth
        self.freed = []

    @property
    def gen(self):
        return len(self.freed)

    def produce_waits(self):
        i = len(self.freed)
        if i < self.depth:
            return []
        return self.freed[i - self.depth]

    def record_free(self, *milestones):
        self.freed.append(list(milestones))


def build_program(debug=0, p1t=0):
    nc = bass.Bass("TRN2", target_bir_lowering=False, debug=False)

    x_in = nc.dram_tensor("x", [XPAD, H], F32, kind="ExternalInput").ap()
    cfgf_in = nc.dram_tensor("cfgf", [128, 4], F32, kind="ExternalInput").ap()
    cfgi_in = nc.dram_tensor("cfgi", [1, 8], I32, kind="ExternalInput").ap()
    Win = nc.dram_tensor("Win", [H, 2 * D], F32, kind="ExternalInput").ap()
    convw_i = nc.dram_tensor("convw", [128, NDT * 3], F32, kind="ExternalInput").ap()
    convb_i = nc.dram_tensor("convb", [128, NDT], F32, kind="ExternalInput").ap()
    Wx = nc.dram_tensor("Wx", [D, 160], F32, kind="ExternalInput").ap()
    Wdt = nc.dram_tensor("Wdt", [R, D], F32, kind="ExternalInput").ap()
    bdt_i = nc.dram_tensor("bdt", [128, NDT], F32, kind="ExternalInput").ap()
    AlogT = nc.dram_tensor("AlogT", [128, NDT * S], F32, kind="ExternalInput").ap()
    Dvec_i = nc.dram_tensor("Dvec", [128, NDT], F32, kind="ExternalInput").ap()
    Wout = nc.dram_tensor("Wout", [D, H], F32, kind="ExternalInput").ap()
    wg8r = nc.dram_tensor("wg8r", [NIT, 128 * NFT * 128], I8, kind="ExternalInput").ap()
    wu8r = nc.dram_tensor("wu8r", [NIT, 128 * NFT * 128], I8, kind="ExternalInput").ap()
    wd8r = nc.dram_tensor("wd8r", [NHT * NE8, 128 * 8 * 128], I8, kind="ExternalInput").ap()
    sgate_i = nc.dram_tensor("sgate", [128, NIT], F32, kind="ExternalInput").ap()
    sup_i = nc.dram_tensor("sup", [128, NIT], F32, kind="ExternalInput").ap()
    sdown_i = nc.dram_tensor("sdown", [128, NHT], F32, kind="ExternalInput").ap()
    identbf = nc.dram_tensor("identbf", [128, 128], BF16, kind="ExternalInput").ap()
    indall = nc.dram_tensor("indall", [32, 32 * 128], BF16, kind="ExternalInput").ap()
    out_t = nc.dram_tensor("out", [TLOC, H], F32, kind="ExternalOutput")

    dt_s = nc.dram_tensor("dt_s", [NDT, 128, TLOC], BF16)
    xc_s = nc.dram_tensor("xc_s", [NDT, 128, TLOC], BF16)
    zs_s = nc.dram_tensor("zs_s", [NDT, 128, TLOC], BF16)
    xnT_s = nc.dram_tensor("xnT_s", [NFT, 128, TLOC], BF16)
    y_s = nc.dram_tensor("y_s", [NDT, 128, TLOC], BF16)
    g_s = nc.dram_tensor("g_s", [NHT, 128, TLOC], BF16)
    bmc_s = nc.dram_tensor("bmc_s", [32, TLOC], BF16)
    send_d = [nc.dram_tensor(f"send{g}", [128, DTG * S], F32) for g in range(NG)]
    recv_d = {cl: [nc.dram_tensor(f"recv{cl}{g}", [256, DTG * S], F32) for g in range(NG)]
              for cl in "ABC"}

    dbg = {}
    if debug >= 1:
        for nm, nt in (("d_dt", NDT), ("d_xc", NDT), ("d_zs", NDT), ("d_xnT", NFT)):
            dbg[nm] = nc.dram_tensor(nm, [nt, 128, TLOC], BF16, kind="ExternalOutput")
        dbg["d_bc"] = nc.dram_tensor("d_bc", [32, TLOC], BF16, kind="ExternalOutput")
    if debug >= 2:
        dbg["d_y"] = nc.dram_tensor("d_y", [NDT, 128, TLOC], BF16, kind="ExternalOutput")
        dbg["d_g"] = nc.dram_tensor("d_g", [NHT, 128, TLOC], BF16, kind="ExternalOutput")

    P = Plan(nc)
    sems = {}

    TNS = ("trm", "mmw", "xdm", "dmm", "bbm", "ffgm", "ffum", "ffdm", "omm", "tr3")
    SCL = ("lns", "dsp", "tre", "evw", "cvs", "aex", "bbe", "ffse", "ffgs", "oae")
    VCT = ("lnv", "cvv", "xdv", "dspv", "scv", "h0v", "ffv", "sgv", "oav")
    SEM_ALIAS = {}
    for n in TNS: SEM_ALIAS[n] = "tns"
    for n in SCL: SEM_ALIAS[n] = "scl"
    for n in VCT: SEM_ALIAS[n] = "vct"

    def SM(name):
        name = SEM_ALIAS.get(name, name)
        if name not in sems:
            sems[name] = Ctr(nc, name)
        return sems[name]

    cc_sems = {(cl, g): nc.alloc_semaphore(f"cc{cl}{g}") for g in range(NG) for cl in "ABC"}

    stack = contextlib.ExitStack()
    ENG = ("tensor", "vector", "scalar", "gpsimd", "sync")

    def sb(st, name, shape, dt):
        return st.enter_context(nc.sbuf_tensor(name, shape, dt))

    def psum(st, name, shape, dt):
        return st.enter_context(nc.psum_tensor(name, shape, dt))

    cpool = sb(stack, "cpool", [128, 8], F32)
    cfgf_t = sb(stack, "cfgf_t", [128, 4], F32)
    cfgi_t = sb(stack, "cfgi_t", [1, 8], I32)
    convw_t = sb(stack, "convw_t", [128, NDT * 3], F32)
    convb_t = sb(stack, "convb_t", [128, NDT], F32)
    bdt_t = sb(stack, "bdt_t", [128, NDT], F32)
    Aneg = sb(stack, "Aneg", [128, NDT * S], F32)
    Dvec_t = sb(stack, "Dvec_t", [128, NDT], F32)
    sgate_t = sb(stack, "sgate_t", [128, NIT], F32)
    sup_t = sb(stack, "sup_t", [128, NIT], F32)
    sdown_t = sb(stack, "sdown_t", [128, NHT], F32)
    ident_t = sb(stack, "ident_t", [128, 128], BF16)
    ind_t = sb(stack, "ind_t", [32, 32 * 128], BF16)
    zeroS = sb(stack, "zeroS", [128, DTG * S], F32)

    CEPS = cpool[:, 0:1]
    CONE = cpool[:, 1:2]
    CMONE = cpool[:, 2:3]

    ld = SM("ld"); gld = SM("gld"); pro = SM("pro"); zz = SM("zz")

    P.op("gpsimd", "memset", cpool[:, 0:1], 1e-5, incs=[zz.nxt()])
    P.op("gpsimd", "memset", cpool[:, 1:2], 1.0, incs=[zz.nxt()])
    P.op("gpsimd", "memset", cpool[:, 2:3], -1.0, incs=[zz.nxt()])
    P.op("gpsimd", "memset", cpool[:, 3:4], 0.0, incs=[zz.nxt()])
    P.op("gpsimd", "memset", cpool[:, 4:5], -0.5, incs=[zz.nxt()])
    P.op("gpsimd", "memset", zeroS[:], 0.0, incs=[zz.nxt()])
    xnt_pad_zz = None
    for g in range(NG):
        P.op("gpsimd", "dma_start", out=send_d[g].ap(), in_=zeroS[:],
             waits=[zz.at()], incs=[gld.nxt(16)])
        for cl in "ABC":
            P.op("gpsimd", "dma_start", out=recv_d[cl][g].ap()[0:128, :], in_=zeroS[:],
                 waits=[zz.at()], incs=[gld.nxt(16)])
            P.op("gpsimd", "dma_start", out=recv_d[cl][g].ap()[128:256, :], in_=zeroS[:],
                 waits=[zz.at()], incs=[gld.nxt(16)])

    for ap_t, src in ((cfgf_t, cfgf_in), (cfgi_t, cfgi_in), (convw_t, convw_i),
                      (convb_t, convb_i), (bdt_t, bdt_i), (Dvec_t, Dvec_i),
                      (sgate_t, sgate_i), (sup_t, sup_i), (sdown_t, sdown_i),
                      (ident_t, identbf), (ind_t, indall), (Aneg, AlogT)):
        P.op("sync", "dma_start", out=ap_t[:], in_=src, incs=[ld.nxt(16)])
    const_done = ld.at()

    P.op("scalar", "activation", Aneg[:], Aneg[:], AF.Exp, waits=[const_done], incs=[pro.nxt()])
    P.op("vector", "tensor_scalar", Aneg[:], Aneg[:], -1.0, None, ALU.mult,
         waits=[pro.at()], incs=[pro.nxt()])
    aneg_ready = pro.at()

    st1 = contextlib.ExitStack()
    wx_hi = sb(st1, "wx_hi", [128, NDT * 128], BF16)
    wx_lo = sb(st1, "wx_lo", [128, NDT * 32], BF16)
    wdt_t = sb(st1, "wdt_t", [128, NDT * 128], BF16)
    xtile = [sb(st1, f"xtile{i}", [128, H], F32) for i in range(2)]
    sqt = sb(st1, "sqt", [128, H], BF16)
    stat = sb(st1, "stat", [128, 8], F32)
    xnbf = [sb(st1, f"xnbf{i}", [128, H], BF16) for i in range(2)]
    xnT = sb(st1, "xnT", [128, NFT * WTOK + 384], BF16)
    NCH = 32
    CTC = 2
    wcol = [sb(st1, f"wcol{i}", [128, NFT * 256], BF16) for i in range(2)]
    xpch = sb(st1, "xpch", [128, CTC * WTOK], BF16)
    xcch = sb(st1, "xcch", [128, CTC * WCORE], BF16)
    convacc = sb(st1, "convacc", [128, WCORE], F32)
    xdbl_hi = sb(st1, "xdbl_hi", [128, WCORE], F32)
    xdbl_lo = sb(st1, "xdbl_lo", [32, WCORE], F32)
    dtinbf = sb(st1, "dtinbf", [128, WCORE], BF16)
    spl1 = [sb(st1, f"spl1{i}", [128, 512], F32) for i in range(2)]
    spl2 = [sb(st1, f"spl2{i}", [128, 512], F32) for i in range(2)]
    dtout = [sb(st1, f"dtout{i}", [128, 512], BF16) for i in range(2)]

    pst1 = contextlib.ExitStack()
    ps_tr = [psum(pst1, f"ps_tr{i}", [128, 128], BF16) for i in range(2)]
    ps_w = [psum(pst1, f"ps_w{i}", [128, 512], F32) for i in range(2)]
    ps_xh = [psum(pst1, f"ps_xh{i}", [128, 512], F32) for i in range(2)]
    ps_xl = [psum(pst1, f"ps_xl{i}", [32, 512], F32) for i in range(2)]

    P.op("gpsimd", "memset", xnT[:, NFT * WTOK: NFT * WTOK + 384], 0.0, incs=[SM("zz").nxt()])
    xls = [SM("xls0"), SM("xls1")]; lnv = SM("lnv"); lns = SM("lns")
    trm = SM("trm"); tre = SM("tre")
    wcl = [SM("wcl0"), SM("wcl1")]; mmw = SM("mmw"); evw = SM("evw")
    cvv = SM("cvv"); cvs = SM("cvs"); xcw = SM("xcw"); zsw = SM("zsw")
    xdm = SM("xdm"); xdv = SM("xdv")
    dmm = SM("dmm"); dsp = SM("dsp"); dspv = SM("dspv")
    dtw = [SM("dtw0"), SM("dtw1")]
    xnw = SM("xnw"); bmcw = SM("bmcw")

    for m in range(NDT):
        P.op("gpsimd", "dma_start",
             out=wx_hi[:, m * 128:(m + 1) * 128],
             in_=Wx[m * 128:(m + 1) * 128, 0:128], incs=[gld.nxt(16)])
        P.op("gpsimd", "dma_start",
             out=wx_lo[:, m * 32:(m + 1) * 32],
             in_=Wx[m * 128:(m + 1) * 128, 128:160], incs=[gld.nxt(16)])
        P.op("gpsimd", "dma_start",
             out=wdt_t[:, m * 128:(m + 1) * 128],
             in_=Wdt[:, m * 128:(m + 1) * 128], incs=[gld.nxt(16)])
    wload_done = gld.at()

    p_xtile = Pipe(2); p_xnbf = Pipe(2); p_trslot = Pipe(2); p_xnT = Pipe(1)
    p_wcol = Pipe(2); p_psw = Pipe(2); p_xpch = Pipe(1); p_xcch = Pipe(1)
    p_xdps = Pipe(1); p_xdbl = Pipe(1); p_dtin = Pipe(1); p_spl = Pipe(2)
    p_dtout = Pipe(2); p_stat = Pipe(1); p_sqt = Pipe(1)

    n_chunks = [(0, 512), (512, 512), (1024, 512), (1536, 512), (2048, 512)]

    for w in range(NW):
        w_t0 = w * WCORE
        x_row0 = w * WCORE
        NT_T = 17
        if p1t == 9: NT_T = 0
        if p1t == 5 or p1t >= 50: NT_T = 2 if w == 0 else 0
        if p1t == 6: NT_T = 17 if w == 0 else 0
        for tt in range(NT_T):
            buf = tt % 2
            P.op("sync", "dma_start", out=xtile[buf][:],
                 in_=x_in[x_row0 + tt * 128: x_row0 + (tt + 1) * 128, :],
                 waits=p_xtile.produce_waits(), incs=[xls[buf].nxt(16)])
            tile_ready = xls[buf].at()
            if p1t == 50:
                continue
            P.op("scalar", "square", sqt[:], xtile[buf][:],
                 waits=[tile_ready] + p_sqt.produce_waits(), incs=[lns.nxt()])
            sq_done = lns.at()
            if p1t == 51:
                continue
            P.op("vector", "reduce_sum", stat[:, 0:1], xtile[buf][:], AX.X,
                 waits=[tile_ready] + p_stat.produce_waits())
            P.op("vector", "reduce_sum", stat[:, 1:2], sqt[:], AX.X,
                 waits=[sq_done], incs=[lnv.nxt()])
            p_sqt.record_free(lnv.at())
            P.op("vector", "tensor_scalar", stat[:, 0:1], stat[:, 0:1], 1.0 / H, None, ALU.mult)
            P.op("vector", "tensor_scalar", stat[:, 1:2], stat[:, 1:2], 1.0 / H, None, ALU.mult)
            P.op("vector", "tensor_mul", stat[:, 2:3], stat[:, 0:1], stat[:, 0:1])
            P.op("vector", "tensor_sub", stat[:, 3:4], stat[:, 1:2], stat[:, 2:3],
                 incs=[lnv.nxt()])
            if p1t == 52:
                continue
            P.op("scalar", "activation", stat[:, 4:5], stat[:, 3:4], AF.Ln, bias=CEPS,
                 waits=[lnv.at(), zz.at()])
            P.op("scalar", "activation", stat[:, 5:6], stat[:, 4:5], AF.Exp,
                 scale=cpool[:, 4:5], incs=[lns.nxt()])
            sqrt_done = lns.at()
            if p1t == 53:
                continue
            P.op("vector", "tensor_scalar", xnbf[buf][:], xtile[buf][:], stat[:, 0:1],
                 stat[:, 5:6], ALU.subtract, ALU.mult,
                 waits=[sqrt_done] + p_xnbf.produce_waits(), incs=[lnv.nxt()])
            xn_ready = lnv.at()
            p_xtile.record_free(xn_ready)
            p_stat.record_free(sqrt_done, xn_ready)
            for ft in range(NFT if p1t != 54 else 0):
                slot = p_trslot.gen % 2
                P.op("tensor", "transpose",
                     ps_tr[slot][:],
                     xnbf[buf][:, ft * 128:(ft + 1) * 128], ident_t[:],
                     waits=[xn_ready, const_done] + p_trslot.produce_waits(),
                     incs=[trm.nxt()])
                P.op("scalar", "copy",
                     xnT[:, ft * WTOK + tt * 128: ft * WTOK + (tt + 1) * 128],
                     ps_tr[slot][:],
                     waits=[trm.at()] + (p_xnT.produce_waits() if (tt == 0 and ft == 0) else []),
                     incs=[tre.nxt()])
                p_trslot.record_free(tre.at())
            p_xnbf.record_free(tre.at())
        trans_done = tre.at()
        for ft in range(NFT if p1t in (0, 1) else 0):
            P.op("sync", "dma_start",
                 out=xnT_s.ap()[ft, :, w_t0:w_t0 + WCORE],
                 in_=xnT[:, ft * WTOK + 1: ft * WTOK + 1 + WCORE],
                 waits=[trans_done], incs=[xnw.nxt(16)])

        for c in range(NCH if p1t == 0 else 0):
            buf = c % 2
            wcw = p_wcol.produce_waits()
            for kt in range(NFT):
                P.op("gpsimd", "dma_start",
                     out=wcol[buf][:, kt * 256:(kt + 1) * 256],
                     in_=Win[kt * 128:(kt + 1) * 128, c * 256:(c + 1) * 256],
                     waits=wcw, incs=[wcl[buf].nxt(16)])
                wcw = []
            wcol_ready = wcl[buf].at()
            is_xp = c < 16
            xpch_waits = p_xpch.produce_waits()
            for ct in range(CTC):
                for (n0, nlen) in n_chunks:
                    elen = min(nlen, WTOK - n0)
                    pw = ps_w[p_psw.gen % 2]
                    psw_waits = p_psw.produce_waits()
                    for kt in range(NFT):
                        P.op("tensor", "matmul", pw[:],
                             wcol[buf][:, kt * 256 + ct * 128: kt * 256 + (ct + 1) * 128],
                             xnT[:, kt * WTOK + n0: kt * WTOK + n0 + nlen],
                             start=(kt == 0), stop=(kt == NFT - 1),
                             waits=([wcol_ready, trans_done, SM("zz").at()] + psw_waits) if kt == 0 else [],
                             incs=([mmw.nxt()] if kt == NFT - 1 else []))
                    mm_done = mmw.at()
                    if is_xp:
                        P.op("scalar", "copy",
                             xpch[:, ct * WTOK + n0: ct * WTOK + n0 + elen],
                             pw[:, 0:elen], waits=[mm_done] + xpch_waits, incs=[evw.nxt()])
                    else:
                        P.op("scalar", "activation",
                             xpch[:, ct * WTOK + n0: ct * WTOK + n0 + elen],
                             pw[:, 0:elen], AF.Silu,
                             waits=[mm_done] + xpch_waits, incs=[evw.nxt()])
                    xpch_waits = []
                    p_psw.record_free(evw.at())
            ch_evac = evw.at()
            if is_xp:
                xcch_waits = p_xcch.produce_waits()
                for dt_i in range(CTC):
                    m = CTC * c + dt_i
                    base = dt_i * WTOK
                    P.op("vector", "tensor_scalar", convacc[:],
                         xpch[:, base + 0: base + WCORE],
                         convw_t[:, m * 3 + 0: m * 3 + 1], None, ALU.mult,
                         waits=[ch_evac, cvs.at()])
                    P.op("vector", "scalar_tensor_tensor", convacc[:],
                         xpch[:, base + 1: base + 1 + WCORE],
                         convw_t[:, m * 3 + 1: m * 3 + 2], convacc[:], ALU.mult, ALU.add)
                    P.op("vector", "scalar_tensor_tensor", convacc[:],
                         xpch[:, base + 2: base + 2 + WCORE],
                         convw_t[:, m * 3 + 2: m * 3 + 3], convacc[:], ALU.mult, ALU.add,
                         incs=[cvv.nxt()])
                    P.op("scalar", "activation", xcch[:, dt_i * WCORE:(dt_i + 1) * WCORE],
                         convacc[:], AF.Silu, bias=convb_t[:, m:m + 1],
                         waits=[cvv.at()] + xcch_waits, incs=[cvs.nxt()])
                    xcch_waits = []
                    P.op("sync", "dma_start", out=xc_s.ap()[m, :, w_t0:w_t0 + WCORE],
                         in_=xcch[:, dt_i * WCORE:(dt_i + 1) * WCORE],
                         waits=[cvs.at()], incs=[xcw.nxt(16)])
                p_xpch.record_free(cvv.at())
                all_silu = cvs.at()
                for nb in range(4):
                    nb0 = nb * 512
                    xps = p_xdps.gen % 2
                    xd_waits = p_xdps.produce_waits()
                    for kt_i in range(CTC):
                        m = CTC * c + kt_i
                        P.op("tensor", "matmul", ps_xh[xps][:],
                             wx_hi[:, m * 128:(m + 1) * 128],
                             xcch[:, kt_i * WCORE + nb0: kt_i * WCORE + nb0 + 512],
                             start=(kt_i == 0), stop=(kt_i == CTC - 1),
                             waits=([all_silu, wload_done] + xd_waits) if kt_i == 0 else [],
                             incs=([xdm.nxt()] if kt_i == CTC - 1 else []))
                    for kt_i in range(CTC):
                        m = CTC * c + kt_i
                        P.op("tensor", "matmul", ps_xl[xps][:],
                             wx_lo[:, m * 32:(m + 1) * 32],
                             xcch[:, kt_i * WCORE + nb0: kt_i * WCORE + nb0 + 512],
                             start=(kt_i == 0), stop=(kt_i == CTC - 1),
                             incs=([xdm.nxt()] if kt_i == CTC - 1 else []))
                    mm_done = xdm.at()
                    xdbl_waits = p_xdbl.produce_waits() if (c == 0 and nb == 0) else []
                    if c == 0:
                        P.op("vector", "tensor_copy", xdbl_hi[:, nb0:nb0 + 512], ps_xh[xps][:],
                             waits=[mm_done] + xdbl_waits)
                        P.op("vector", "tensor_copy", xdbl_lo[:, nb0:nb0 + 512], ps_xl[xps][:],
                             incs=[xdv.nxt()])
                    else:
                        P.op("vector", "tensor_add", xdbl_hi[:, nb0:nb0 + 512],
                             xdbl_hi[:, nb0:nb0 + 512], ps_xh[xps][:], waits=[mm_done])
                        P.op("vector", "tensor_add", xdbl_lo[:, nb0:nb0 + 512],
                             xdbl_lo[:, nb0:nb0 + 512], ps_xl[xps][:], incs=[xdv.nxt()])
                    p_xdps.record_free(xdv.at())
                p_xcch.record_free(xdv.at(), xcw.at())
            else:
                for dt_i in range(CTC):
                    m = CTC * (c - 16) + dt_i
                    P.op("sync", "dma_start", out=zs_s.ap()[m, :, w_t0:w_t0 + WCORE],
                         in_=xpch[:, dt_i * WTOK + 1: dt_i * WTOK + 1 + WCORE],
                         waits=[ch_evac], incs=[zsw.nxt(16)])
                p_xpch.record_free(zsw.at())
            p_wcol.record_free(mmw.at())
        if p1t != 0:
            p_xnT.record_free(mmw.at(), xnw.at())
            p_xdbl.record_free(xdv.at())
            p_dtin.record_free(dmm.at())
            continue
        xdbl_done = xdv.at()
        P.op("gpsimd", "dma_start", out=bmc_s.ap()[:, w_t0:w_t0 + WCORE], in_=xdbl_lo[:],
             waits=[xdbl_done], incs=[bmcw.nxt(16)])
        dtin_waits = p_dtin.produce_waits()
        P.op("scalar", "copy", dtinbf[:], xdbl_hi[:],
             waits=[xdbl_done] + dtin_waits, incs=[dsp.nxt()])
        dtin_ready = dsp.at()
        for m in range(NDT):
            for hf in range(4):
                ob = p_psw.gen % 2
                psw_waits = p_psw.produce_waits()
                P.op("tensor", "matmul", ps_w[ob][:],
                     wdt_t[:, m * 128:(m + 1) * 128], dtinbf[:, hf * 512:(hf + 1) * 512],
                     start=True, stop=True,
                     waits=[dtin_ready, wload_done] + psw_waits, incs=[dmm.nxt()])
                mm_done = dmm.at()
                sb_i = p_spl.gen % 2
                spl_waits = p_spl.produce_waits()
                P.op("scalar", "activation", spl1[sb_i][:], ps_w[ob][:], AF.Abs,
                     bias=bdt_t[:, m:m + 1], waits=[mm_done, const_done, zz.at()] + spl_waits)
                P.op("scalar", "activation", spl2[sb_i][:], spl1[sb_i][:], AF.Exp, scale=CMONE)
                P.op("scalar", "activation", spl1[sb_i][:], spl2[sb_i][:], AF.Ln, bias=CONE)
                P.op("scalar", "activation", spl2[sb_i][:], ps_w[ob][:], AF.Relu,
                     bias=bdt_t[:, m:m + 1], incs=[dsp.nxt()])
                p_psw.record_free(dsp.at())
                do_i = p_dtout.gen % 2
                dt_waits = p_dtout.produce_waits()
                P.op("vector", "tensor_add", dtout[do_i][:], spl1[sb_i][:], spl2[sb_i][:],
                     waits=[dsp.at()] + dt_waits, incs=[dspv.nxt()])
                p_spl.record_free(dspv.at())
                P.op("sync", "dma_start",
                     out=dt_s.ap()[m, :, w_t0 + hf * 512: w_t0 + (hf + 1) * 512],
                     in_=dtout[do_i][:], waits=[dspv.at()], incs=[dtw[do_i].nxt(16)])
                p_dtout.record_free(dtw[do_i].at())
        p_dtin.record_free(dmm.at())
        p_xdbl.record_free(dmm.at(), bmcw.at())
        p_xnT.record_free(mmw.at(), xnw.at())

    p1_bar = [dtw[0].at(), dtw[1].at(), xcw.at(), zsw.at(), xnw.at(), bmcw.at(), dmm.at(), dspv.at(),
              dsp.at(), wcl[0].at(), wcl[1].at(), aneg_ready]

    if debug == 1:
        dd = SM("dbgd")
        dump_list = (("d_dt", dt_s), ("d_xc", xc_s), ("d_zs", zs_s), ("d_xnT", xnT_s))
        if p1t in (5, 6, 7, 9):
            dump_list = ()
        if p1t == 1:
            dump_list = (("d_xnT", xnT_s),)
        for nm, src in dump_list:
            P.op("gpsimd", "dma_start", out=dbg[nm].ap(), in_=src.ap(),
                 waits=p1_bar, incs=[dd.nxt(16)])
        if p1t == 0:
            P.op("gpsimd", "dma_start", out=dbg["d_bc"].ap(), in_=bmc_s.ap(),
                 waits=p1_bar, incs=[dd.nxt(16)])
        P.op("gpsimd", "memset", cpool[:, 7:8], 1.0, waits=[dd.at()])
        finish_plan(nc, P)
        return nc, stack, dbg

    st1.close()
    pst1.close()

    st2 = contextlib.ExitStack()
    IN = {}
    for t in ("dt", "xc", "zs"):
        IN[t] = [[sb(st2, f"in_{t}{mi}{bi}", [128, TB2], BF16) for bi in range(2)]
                 for mi in range(DTG)]
    bmcsl = [sb(st2, f"bmcsl{bi}", [32, TB2], BF16) for bi in range(2)]
    a_t = [sb(st2, f"a_t{i}", [128, TB2], F32) for i in range(4)]
    b_t = [sb(st2, f"b_t{i}", [128, TB2], BF16) for i in range(4)]
    h_t = [sb(st2, f"h_t{i}", [128, TB2], BF16) for i in range(4)]
    hc_t = [sb(st2, f"hc_t{i}", [128, TB2], BF16) for i in range(2)]
    bb_t = [sb(st2, f"bb_t{i}", [128, TB2], BF16) for i in range(2)]
    cb_t = [sb(st2, f"cb_t{i}", [128, TB2], BF16) for i in range(2)]
    dtu_t = [sb(st2, f"dtu_t{i}", [128, TB2], BF16) for i in range(DTG)]
    y_t = [sb(st2, f"y_t{i}", [128, TB2], F32) for i in range(DTG)]
    yf_t = [sb(st2, f"yf_t{i}", [128, TB2], BF16) for i in range(2)]
    stbuf = sb(st2, "stbuf", [128, DTG * S], F32)
    h0t = sb(st2, "h0t", [128, DTG * S], F32)
    cand = {cl: sb(st2, f"cand{cl}", [128, DTG * S], F32) for cl in "ABC"}
    xnTb = sb(st2, "xnTb", [128, NFT * FFB], BF16)
    wg_t = [sb(st2, f"wg_t{i}", [128, NFT * 128], BF16) for i in range(2)]
    wu_t = [sb(st2, f"wu_t{i}", [128, NFT * 128], BF16) for i in range(2)]
    wdn_t = [sb(st2, f"wdn_t{i}", [128, 8 * 128], BF16) for i in range(2)]
    fuse = sb(st2, "fuse", [128, 8 * FFB], BF16)
    dacc = sb(st2, "dacc", [128, NHT * FFB], F32)
    gs_t = [sb(st2, f"gs_t{i}", [128, FFB], BF16) for i in range(2)]
    us_t = [sb(st2, f"us_t{i}", [128, FFB], BF16) for i in range(2)]
    gout = [sb(st2, f"gout{i}", [128, FFB], BF16) for i in range(2)]

    pst2 = contextlib.ExitStack()
    ps_bb = [psum(pst2, f"ps_bb{i}", [128, 512], F32) for i in range(2)]
    ps_g = [psum(pst2, f"ps_g{i}", [128, FFB], F32) for i in range(2)]
    ps_u = [psum(pst2, f"ps_u{i}", [128, FFB], F32) for i in range(2)]
    ps_d = [psum(pst2, f"ps_d{i}", [128, FFB], F32) for i in range(2)]

    gp_regs = {nm: nc.gpsimd.alloc_register(nm) for nm in ("sA", "sB", "sC", "rA", "rB", "rC")}
    sy_regs = {nm: nc.sync.alloc_register(nm) for nm in ("wA", "wB", "wC")}

    sld = [SM("sld0"), SM("sld1")]; bsl = [SM("bsl0"), SM("bsl1")]; aex = SM("aex")
    bbm = SM("bbm"); bbe = SM("bbe"); scv = SM("scv")
    ydm = [SM("ydm0"), SM("ydm1")]; sdma = SM("sdma"); rcv = SM("rcv"); h0v = SM("h0v")
    ffx = SM("ffx")
    wgs = [SM("wgs0"), SM("wgs1")]; wus = [SM("wus0"), SM("wus1")]
    wdns = [SM("wdns0"), SM("wdns1")]
    ffgm = SM("ffgm"); ffum = SM("ffum"); ffse = SM("ffse"); ffv = SM("ffv")
    ffdm = SM("ffdm"); ffda = SM("ffda"); ffgs = SM("ffgs"); gw = SM("gw")

    p2first = {e: True for e in ENG}

    def bar(eng, waits):
        if p2first[eng]:
            p2first[eng] = False
            return list(waits) + p1_bar
        return list(waits)

    def op2(eng, method, *args, waits=(), incs=(), **kw):
        P.op(eng, method, *args, waits=bar(eng, waits), incs=incs, **kw)

    op2("gpsimd", "load", gp_regs["sA"], cfgi_t[0:1, 0:1], waits=[const_done])
    op2("gpsimd", "load", gp_regs["sB"], cfgi_t[0:1, 1:2])
    op2("gpsimd", "load", gp_regs["sC"], cfgi_t[0:1, 2:3])
    op2("gpsimd", "reg_mov", gp_regs["rA"], 0)
    op2("gpsimd", "reg_mov", gp_regs["rB"], 0)
    op2("gpsimd", "reg_mov", gp_regs["rC"], 0)
    op2("sync", "load", sy_regs["wA"], cfgi_t[0:1, 3:4], waits=[const_done])
    op2("sync", "load", sy_regs["wB"], cfgi_t[0:1, 4:5])
    op2("sync", "load", sy_regs["wC"], cfgi_t[0:1, 5:6])

    ffn_thunks = []
    p_xnTb = Pipe(1); p_wg = Pipe(2); p_wu = Pipe(2); p_wdn = Pipe(2)
    p_fuse = Pipe(1); p_psg = Pipe(1); p_psu = Pipe(1); p_psd = Pipe(1)
    p_gsu = Pipe(2); p_dacc = Pipe(1); p_gout = Pipe(2)

    def ffn_block(f):
        f0 = f * FFB

        def load_xnTb():
            xw = p_xnTb.produce_waits()
            for ft in range(NFT):
                op2("gpsimd", "dma_start",
                    out=xnTb[:, ft * FFB:(ft + 1) * FFB],
                    in_=xnT_s.ap()[ft, :, f0:f0 + FFB],
                    waits=xw, incs=[ffx.nxt(16)])
                xw = []
        ffn_thunks.append(load_xnTb)
        for e in range(NE8):
            for il in range(8):
                it = e * 8 + il

                def gu(it=it, il=il):
                    wbg = p_wg.gen % 2
                    wg_waits = p_wg.produce_waits()
                    op2("gpsimd", "dma_start",
                        out=wg_t[wbg][:],
                        in_=wg8r[it:it + 1, :].rearrange("o (p kc) -> p (o kc)", p=128),
                        waits=wg_waits, incs=[wgs[wbg].nxt(16)])
                    wbu = p_wu.gen % 2
                    wu_waits = p_wu.produce_waits()
                    op2("gpsimd", "dma_start",
                        out=wu_t[wbu][:],
                        in_=wu8r[it:it + 1, :].rearrange("o (p kc) -> p (o kc)", p=128),
                        waits=wu_waits, incs=[wus[wbu].nxt(16)])
                    wg_ready = wgs[wbg].at()
                    wu_ready = wus[wbu].at()
                    psg_i = p_psg.gen % 2
                    psg_waits = p_psg.produce_waits()
                    for kt in range(NFT):
                        op2("tensor", "matmul", ps_g[psg_i][:],
                            wg_t[wbg][:, kt * 128:(kt + 1) * 128],
                            xnTb[:, kt * FFB:(kt + 1) * FFB],
                            start=(kt == 0), stop=(kt == NFT - 1),
                            waits=([wg_ready, (ffx.h, ffx.n)] + psg_waits) if kt == 0 else [],
                            incs=([ffgm.nxt()] if kt == NFT - 1 else []))
                    psu_i = p_psu.gen % 2
                    psu_waits = p_psu.produce_waits()
                    for kt in range(NFT):
                        op2("tensor", "matmul", ps_u[psu_i][:],
                            wu_t[wbu][:, kt * 128:(kt + 1) * 128],
                            xnTb[:, kt * FFB:(kt + 1) * FFB],
                            start=(kt == 0), stop=(kt == NFT - 1),
                            waits=([wu_ready] + psu_waits) if kt == 0 else [],
                            incs=([ffum.nxt()] if kt == NFT - 1 else []))
                    p_wg.record_free(ffgm.at())
                    p_wu.record_free(ffum.at())
                    gsb = p_gsu.gen % 2
                    gsu_waits = p_gsu.produce_waits()
                    op2("scalar", "activation", gs_t[gsb][:], ps_g[psg_i][:], AF.Silu,
                        scale=sgate_t[:, it:it + 1],
                        waits=[ffgm.at()] + gsu_waits, incs=[ffse.nxt()])
                    op2("scalar", "mul", us_t[gsb][:], ps_u[psu_i][:], sup_t[:, it:it + 1],
                        waits=[ffum.at()], incs=[ffse.nxt()])
                    p_psg.record_free(ffse.at())
                    p_psu.record_free(ffse.at())
                    fu_waits = p_fuse.produce_waits() if il == 0 else []
                    op2("vector", "tensor_mul", fuse[:, il * FFB:(il + 1) * FFB],
                        gs_t[gsb][:], us_t[gsb][:],
                        waits=[ffse.at()] + fu_waits, incs=[ffv.nxt()])
                    p_gsu.record_free(ffv.at())
                ffn_thunks.append(gu)
            for ht in range(NHT):
                def dn(e=e, ht=ht):
                    wbd = p_wdn.gen % 2
                    wdn_waits = p_wdn.produce_waits()
                    op2("gpsimd", "dma_start",
                        out=wdn_t[wbd][:],
                        in_=wd8r[ht * NE8 + e: ht * NE8 + e + 1, :].rearrange(
                            "o (p kc) -> p (o kc)", p=128),
                        waits=wdn_waits, incs=[wdns[wbd].nxt(16)])
                    wd_ready = wdns[wbd].at()
                    psd_i = p_psd.gen % 2
                    psd_waits = p_psd.produce_waits()
                    for j in range(8):
                        op2("tensor", "matmul", ps_d[psd_i][:],
                            wdn_t[wbd][:, j * 128:(j + 1) * 128],
                            fuse[:, j * FFB:(j + 1) * FFB],
                            start=(j == 0), stop=(j == 7),
                            waits=([wd_ready, ffv.at()] + psd_waits) if j == 0 else [],
                            incs=([ffdm.nxt()] if j == 7 else []))
                    p_wdn.record_free(ffdm.at())
                    da_waits = p_dacc.produce_waits() if (e == 0 and ht == 0) else []
                    if e == 0:
                        op2("scalar", "copy", dacc[:, ht * FFB:(ht + 1) * FFB], ps_d[psd_i][:],
                            waits=[ffdm.at()] + da_waits, incs=[ffda.nxt()])
                    else:
                        op2("vector", "tensor_add", dacc[:, ht * FFB:(ht + 1) * FFB],
                            dacc[:, ht * FFB:(ht + 1) * FFB], ps_d[psd_i][:],
                            waits=[ffdm.at()], incs=[ffda.nxt()])
                    p_psd.record_free(ffda.at())
                ffn_thunks.append(dn)

            def endeighth():
                p_fuse.record_free(ffdm.at())
            ffn_thunks.append(endeighth)
        for ht in range(NHT):
            def gsil(ht=ht, f0=f0):
                gob = p_gout.gen % 2
                go_waits = p_gout.produce_waits()
                op2("scalar", "activation", gout[gob][:], dacc[:, ht * FFB:(ht + 1) * FFB],
                    AF.Silu, scale=sdown_t[:, ht:ht + 1],
                    waits=[ffda.at()] + go_waits, incs=[ffgs.nxt()])
                op2("sync", "dma_start", out=g_s.ap()[ht, :, f0:f0 + FFB],
                    in_=gout[gob][:], waits=[ffgs.at(), gw.at()], incs=[gw.nxt(16)])
                p_gout.record_free(gw.at())
            ffn_thunks.append(gsil)

        def endblock():
            p_dacc.record_free(ffgs.at())
            p_xnTb.record_free(ffgm.at(), ffum.at())
        ffn_thunks.append(endblock)

    for f in range(NFB):
        ffn_block(f)
    ffn_per_group = (len(ffn_thunks) + NG - 1) // NG
    ffn_pos = 0

    p_a = Pipe(4); p_b = Pipe(4); p_h = Pipe(4); p_hc = Pipe(2)
    p_bbcb = Pipe(2); p_psbb = Pipe(2)
    p_dtu = [Pipe(1) for _ in range(DTG)]
    p_y = [Pipe(1) for _ in range(DTG)]
    p_yf = Pipe(2)
    p_stb = Pipe(1); p_h0 = Pipe(1); p_cand = Pipe(1)

    def t2_free_wait(glob_i):
        if glob_i < 2:
            return []
        return [(ydm[glob_i % 2].h, (glob_i // 2) * 32)]

    for g in range(NG):
        for _ in range(ffn_per_group):
            if ffn_pos < len(ffn_thunks):
                ffn_thunks[ffn_pos]()
                ffn_pos += 1

        cnd_waits = p_cand.produce_waits()
        for ci, cl in enumerate("ABC"):
            op2("sync", "wait_ge", cc_sems[(cl, g)], sy_regs["w" + cl])
            op2("sync", "dma_start", out=cand[cl][:], in_=recv_d[cl][g].ap()[0:128, :],
                waits=cnd_waits if ci == 0 else [], incs=[rcv.nxt(16)])
            cnd_waits = []
        recv_done = rcv.at()
        h0_waits = p_h0.produce_waits()
        op2("vector", "tensor_scalar", h0t[:], cand["A"][:], cfgf_t[:, 0:1], None, ALU.mult,
            waits=[recv_done, const_done] + h0_waits)
        op2("vector", "scalar_tensor_tensor", h0t[:], cand["B"][:], cfgf_t[:, 1:2], h0t[:],
            ALU.mult, ALU.add)
        op2("vector", "scalar_tensor_tensor", h0t[:], cand["C"][:], cfgf_t[:, 2:3], h0t[:],
            ALU.mult, ALU.add, incs=[h0v.nxt()])
        h0_ready = h0v.at()
        p_cand.record_free(h0_ready)

        sl_ready_list = []
        bmc_ready_list = []
        for t2 in range(NT2):
            gi = g * NT2 + t2
            fw = t2_free_wait(gi)
            for mi in range(DTG):
                m = DTG * g + mi
                for t, src in (("dt", dt_s), ("xc", xc_s), ("zs", zs_s)):
                    op2("sync", "dma_start",
                        out=IN[t][mi][t2 % 2][:],
                        in_=src.ap()[m, :, t2 * TB2:(t2 + 1) * TB2],
                        waits=fw, incs=[sld[gi % 2].nxt(16)])
                    fw = []
            op2("sync", "dma_start", out=bmcsl[t2 % 2][:],
                in_=bmc_s.ap()[:, t2 * TB2:(t2 + 1) * TB2], incs=[bsl[gi % 2].nxt(16)])
            sl_ready_list.append(sld[gi % 2].at())
            bmc_ready_list.append(bsl[gi % 2].at())

        for t2 in range(NT2):
            sl_ready = sl_ready_list[t2]
            bmc_ready = bmc_ready_list[t2]
            bmc_buf = bmcsl[t2 % 2]
            for mi in range(DTG):
                dtu_waits = p_dtu[mi].produce_waits()
                op2("vector", "tensor_mul", dtu_t[mi][:],
                    IN["dt"][mi][t2 % 2][:], IN["xc"][mi][t2 % 2][:],
                    waits=[sl_ready] + dtu_waits, incs=[scv.nxt()])
            dtu_ready = scv.at()
            for s in range(S):
                bbuf = p_bbcb.gen % 2
                bb_waits = p_bbcb.produce_waits()
                first_w = [bmc_ready, const_done]
                for dst_t, jrow in ((bb_t, s), (cb_t, 16 + s)):
                    for hh in range(2):
                        pb = ps_bb[p_psbb.gen % 2]
                        psbbw = p_psbb.produce_waits()
                        op2("tensor", "matmul", pb[:],
                            ind_t[:, jrow * 128:(jrow + 1) * 128],
                            bmc_buf[:, hh * 512:(hh + 1) * 512], start=True, stop=True,
                            waits=first_w + psbbw, incs=[bbm.nxt()])
                        first_w = []
                        op2("scalar", "copy", dst_t[bbuf][:, hh * 512:(hh + 1) * 512], pb[:],
                            waits=[bbm.at()] + (bb_waits if (jrow == s and hh == 0) else []),
                            incs=[bbe.nxt()])
                        p_psbb.record_free(bbe.at())
                bbcb_ready = bbe.at()
                for mi in range(DTG):
                    m = DTG * g + mi
                    lane = mi * S + s
                    ab = p_a.gen % 4
                    a_waits = p_a.produce_waits()
                    op2("scalar", "activation", a_t[ab][:], IN["dt"][mi][t2 % 2][:],
                        AF.Exp, scale=Aneg[:, m * S + s: m * S + s + 1],
                        waits=[sl_ready, aneg_ready] + a_waits, incs=[aex.nxt()])
                    a_ready = aex.at()
                    bb2 = p_b.gen % 4
                    b_waits = p_b.produce_waits()
                    op2("vector", "tensor_mul", b_t[bb2][:], dtu_t[mi][:], bb_t[bbuf][:],
                        waits=[dtu_ready, bbcb_ready] + b_waits, incs=[scv.nxt()])
                    hb = p_h.gen % 4
                    h_waits = p_h.produce_waits()
                    init_ap = h0t[:, lane:lane + 1] if t2 == 0 else stbuf[:, lane:lane + 1]
                    stb_waits = p_stb.produce_waits() if (t2 == 0 and s == 0 and mi == 0) else []
                    op2("vector", "tensor_tensor_scan", h_t[hb][:], a_t[ab][:], b_t[bb2][:],
                        init_ap, ALU.mult, ALU.add,
                        waits=[a_ready] + h_waits + ([h0_ready] if t2 == 0 else []) + stb_waits,
                        incs=[scv.nxt()])
                    p_a.record_free(scv.at())
                    p_b.record_free(scv.at())
                    op2("vector", "tensor_copy", stbuf[:, lane:lane + 1],
                        h_t[hb][:, TB2 - 1:TB2], incs=[scv.nxt()])
                    if s == 0:
                        op2("vector", "tensor_mul", y_t[mi][:], h_t[hb][:], cb_t[bbuf][:],
                            waits=p_y[mi].produce_waits(), incs=[scv.nxt()])
                        p_h.record_free(scv.at())
                    else:
                        hcb = p_hc.gen % 2
                        hc_waits = p_hc.produce_waits()
                        op2("vector", "tensor_mul", hc_t[hcb][:], h_t[hb][:], cb_t[bbuf][:],
                            waits=hc_waits, incs=[scv.nxt()])
                        p_h.record_free(scv.at())
                        op2("vector", "tensor_add", y_t[mi][:], y_t[mi][:], hc_t[hcb][:],
                            incs=[scv.nxt()])
                        p_hc.record_free(scv.at())
                p_bbcb.record_free(scv.at())
            gi2 = g * NT2 + t2
            for mi in range(DTG):
                m = DTG * g + mi
                yfb = p_yf.gen % 2
                yf_waits = p_yf.produce_waits()
                op2("vector", "scalar_tensor_tensor", y_t[mi][:],
                    IN["xc"][mi][t2 % 2][:], Dvec_t[:, m:m + 1], y_t[mi][:],
                    ALU.mult, ALU.add, waits=[const_done])
                op2("vector", "tensor_mul", yf_t[yfb][:], y_t[mi][:],
                    IN["zs"][mi][t2 % 2][:], waits=yf_waits, incs=[scv.nxt()])
                p_y[mi].record_free(scv.at())
                p_dtu[mi].record_free(scv.at())
                op2("scalar", "dma_start", out=y_s.ap()[m, :, t2 * TB2:(t2 + 1) * TB2],
                    in_=yf_t[yfb][:], waits=[scv.at()], incs=[ydm[gi2 % 2].nxt(16)])
                p_yf.record_free(ydm[0].at(), ydm[1].at())
        op2("scalar", "dma_start", out=send_d[g].ap(), in_=stbuf[:],
            waits=[scv.at(), sdma.at()], incs=[sdma.nxt(16)])
        p_stb.record_free(sdma.at())
        p_h0.record_free(scv.at())

        op2("gpsimd", "reg_add", gp_regs["rA"], gp_regs["rA"], gp_regs["sA"])
        op2("gpsimd", "reg_add", gp_regs["rB"], gp_regs["rB"], gp_regs["sB"])
        op2("gpsimd", "reg_add", gp_regs["rC"], gp_regs["rC"], gp_regs["sC"])
        op2("gpsimd", "wait_ge", sdma.h, gp_regs["rA"])
        op2("gpsimd", "collective_compute", "AllGather", ALU.bypass,
            replica_groups=RG_A, ins=[send_d[g].ap().opt()],
            outs=[recv_d["A"][g].ap().opt()], incs=[(cc_sems[("A", g)], None)])
        op2("gpsimd", "wait_ge", sdma.h, gp_regs["rB"])
        op2("gpsimd", "collective_compute", "AllGather", ALU.bypass,
            replica_groups=RG_B, ins=[send_d[g].ap().opt()],
            outs=[recv_d["B"][g].ap().opt()], incs=[(cc_sems[("B", g)], None)])
        op2("gpsimd", "wait_ge", sdma.h, gp_regs["rC"])
        op2("gpsimd", "collective_compute", "AllGather", ALU.bypass,
            replica_groups=RG_C, ins=[send_d[g].ap().opt()],
            outs=[recv_d["C"][g].ap().opt()], incs=[(cc_sems[("C", g)], None)])

    while ffn_pos < len(ffn_thunks):
        ffn_thunks[ffn_pos]()
        ffn_pos += 1

    p2_bar = [scv.at(), ydm[0].at(), ydm[1].at(), sdma.at(), gw.at(), ffgs.at(),
              ffv.at(), ffda.at(), bbe.at(), aex.at(), bbm.at(), ffdm.at(),
              sld[0].at(), sld[1].at(), rcv.at(), bsl[0].at(), bsl[1].at(),
              ffx.at(), wgs[0].at(), wgs[1].at(), wus[0].at(), wus[1].at(),
              wdns[0].at(), wdns[1].at(), h0v.at()]

    if debug == 2:
        dd = SM("dbgd")
        for nm, src in (("d_dt", dt_s), ("d_xc", xc_s), ("d_zs", zs_s), ("d_xnT", xnT_s)):
            P.op("gpsimd", "dma_start", out=dbg[nm].ap(), in_=src.ap(),
                 waits=p2_bar, incs=[dd.nxt(16)])
        P.op("gpsimd", "dma_start", out=dbg["d_bc"].ap(), in_=bmc_s.ap(),
             waits=p2_bar, incs=[dd.nxt(16)])
        P.op("gpsimd", "dma_start", out=dbg["d_y"].ap(), in_=y_s.ap(),
             waits=p2_bar, incs=[dd.nxt(16)])
        P.op("gpsimd", "dma_start", out=dbg["d_g"].ap(), in_=g_s.ap(),
             waits=p2_bar, incs=[dd.nxt(16)])
        P.op("gpsimd", "memset", cpool[:, 7:8], 1.0, waits=[dd.at()])
        finish_plan(nc, P)
        return nc, stack, dbg

    st2.close()
    pst2.close()

    st3 = contextlib.ExitStack()
    wout_res = sb(st3, "wout_res", [128, NHT * NDT * 128], BF16)
    ybk = sb(st3, "ybk", [128, NDT * P3B], BF16)
    gbk = sb(st3, "gbk", [128, NHT * P3B], BF16)
    sgb = [sb(st3, f"sgb{i}", [128, P3B], BF16) for i in range(2)]
    oasm = [sb(st3, f"oasm{i}", [128, H], BF16) for i in range(P3B // 128)]
    xt3 = [sb(st3, f"xt3{i}", [128, H], F32) for i in range(1)]
    outf = [sb(st3, f"outf{i}", [128, H], F32) for i in range(1)]

    pst3 = contextlib.ExitStack()
    ps_o = [psum(pst3, f"ps_o{i}", [128, P3B], F32) for i in range(2)]
    ps_t3 = [psum(pst3, f"ps_t3{i}", [128, 128], BF16) for i in range(2)]

    wol = SM("wol"); ybl = SM("ybl"); gbl = SM("gbl"); x3l = SM("x3l")
    omm = SM("omm"); sgv = SM("sgv"); tr3 = SM("tr3"); oae = SM("oae")
    oav = SM("oav"); ost = SM("ost")

    p3first = {e: True for e in ENG}

    def bar3(eng, waits):
        if p3first[eng]:
            p3first[eng] = False
            return list(waits) + p2_bar
        return list(waits)

    def op3(eng, method, *args, waits=(), incs=(), **kw):
        P.op(eng, method, *args, waits=bar3(eng, waits), incs=incs, **kw)

    for ht in range(NHT):
        for part in range(16):
            k0 = part * 2
            op3("gpsimd", "dma_start",
                out=wout_res[:, ht * NDT * 128 + k0 * 128: ht * NDT * 128 + (k0 + 2) * 128]
                    .rearrange("p (k c) -> p k c", c=128),
                in_=Wout[k0 * 128:(k0 + 2) * 128, ht * 128:(ht + 1) * 128]
                    .rearrange("(k p) c -> p k c", p=128),
                incs=[wol.nxt(16)])
    wout_ready = wol.at()

    p_ybk = Pipe(1); p_gbk = Pipe(1); p_sgb = Pipe(2); p_oasm = Pipe(1)
    p_xt3 = Pipe(1); p_outf = Pipe(1); p_pso = Pipe(2); p_pst3 = Pipe(2)

    for tb in range(NP3):
        t0 = tb * P3B
        yw = [ydm[0].at(), ydm[1].at()] + p_ybk.produce_waits()
        for m in range(NDT):
            op3("gpsimd", "dma_start", out=ybk[:, m * P3B:(m + 1) * P3B],
                in_=y_s.ap()[m, :, t0:t0 + P3B], waits=yw, incs=[ybl.nxt(16)])
            yw = []
        ybk_ready = ybl.at()
        gw2 = [gw.at()] + p_gbk.produce_waits()
        for ht2 in range(NHT):
            op3("gpsimd", "dma_start", out=gbk[:, ht2 * P3B:(ht2 + 1) * P3B],
                in_=g_s.ap()[ht2, :, t0:t0 + P3B], waits=gw2, incs=[gbl.nxt(16)])
            gw2 = []
        gbk_ready = gbl.at()
        oasm_waits = p_oasm.produce_waits()
        for ht in range(NHT):
            po = ps_o[p_pso.gen % 2]
            pso_waits = p_pso.produce_waits()
            for kt in range(NDT):
                op3("tensor", "matmul", po[:],
                    wout_res[:, ht * NDT * 128 + kt * 128: ht * NDT * 128 + (kt + 1) * 128],
                    ybk[:, kt * P3B:(kt + 1) * P3B],
                    start=(kt == 0), stop=(kt == NDT - 1),
                    waits=([wout_ready, ybk_ready] + pso_waits) if kt == 0 else [],
                    incs=([omm.nxt()] if kt == NDT - 1 else []))
            sgbuf = p_sgb.gen % 2
            sg_waits = p_sgb.produce_waits()
            op3("vector", "tensor_mul", sgb[sgbuf][:], po[:],
                gbk[:, ht * P3B:(ht + 1) * P3B],
                waits=[omm.at(), gbk_ready] + sg_waits, incs=[sgv.nxt()])
            p_pso.record_free(sgv.at())
            sg_ready = sgv.at()
            for j in range(P3B // 128):
                slot = p_pst3.gen % 2
                pt_waits = p_pst3.produce_waits()
                op3("tensor", "transpose", ps_t3[slot][:],
                    sgb[sgbuf][:, j * 128:(j + 1) * 128], ident_t[:],
                    waits=[sg_ready] + pt_waits, incs=[tr3.nxt()])
                op3("scalar", "copy", oasm[j][:, ht * 128:(ht + 1) * 128],
                    ps_t3[slot][:],
                    waits=[tr3.at()] + (oasm_waits if (ht == 0 and j == 0) else []),
                    incs=[oae.nxt()])
                p_pst3.record_free(oae.at())
            p_sgb.record_free(oae.at())
        asm_done = oae.at()
        p_ybk.record_free(omm.at())
        p_gbk.record_free(sgv.at())
        for j in range(P3B // 128):
            xb = 0
            xw = p_xt3.produce_waits()
            op3("sync", "dma_start", out=xt3[xb][:],
                in_=x_in[1 + t0 + j * 128: 1 + t0 + (j + 1) * 128, :],
                waits=xw, incs=[x3l.nxt(16)])
            ob = 0
            ow = p_outf.produce_waits()
            op3("vector", "tensor_add", outf[ob][:], xt3[xb][:], oasm[j][:],
                waits=[asm_done, x3l.at()] + ow, incs=[oav.nxt()])
            p_xt3.record_free(oav.at())
            op3("sync", "dma_start", out=out_t.ap()[t0 + j * 128: t0 + (j + 1) * 128, :],
                in_=outf[ob][:], waits=[oav.at()], incs=[ost.nxt(16)])
            p_outf.record_free(ost.at())
        p_oasm.record_free(oav.at())

    P.op("gpsimd", "wait_ge", ost.h, ost.n)

    finish_plan(nc, P)
    return nc, stack, dbg


def finish_plan(nc, P):
    with nc.Block() as block:
        @block.tensor
        def _(e):
            P.replay("tensor", e)

        @block.vector
        def _(e):
            P.replay("vector", e)

        @block.scalar
        def _(e):
            P.replay("scalar", e)

        @block.gpsimd
        def _(e):
            P.replay("gpsimd", e)

        @block.sync
        def _(e):
            P.replay("sync", e)


def kernel(**inputs):
    in_maps = stage_inputs(inputs)
    nc, stack, _ = build_program(debug=0)
    res = run_bass_kernel_spmd(nc, in_maps, core_ids=list(range(NC)))
    out = np.empty((B, L, H), np.float32)
    for k in range(NC):
        b, q = k // 4, k % 4
        out[b, q * TLOC:(q + 1) * TLOC, :] = res.results[k]["out"]
    return out
